# version 03
# baseline (speedup 1.0000x reference)
import numpy as np
import ml_dtypes
from contextlib import ExitStack

import concourse.bass as bass
import concourse.mybir as mybir
from concourse.bass_utils import run_bass_kernel_spmd

F32 = mybir.dt.float32
BF16 = mybir.dt.bfloat16
I32 = mybir.dt.int32
AF = mybir.ActivationFunctionType
ALU = mybir.AluOpType
AX = mybir.AxisListType

P = 128
D = 2048
S_FULL = 4096
S_OWN = 2048
H = 16
HD = 128
M = 4
HID = 5632
NH = HID // 128
ND = D // 128
NT = S_FULL // 128
NTO = S_OWN // 128
CH = 32
CHO = 16
EPS = 1e-6
LN_EPS = 1e-5
SCALE = HD ** -0.5
BIG = 1e30

PSUM_BANKS = {"pb0": 0, "pb1": 1, "pb2": 2, "pb3": 3, "pb4": 4,
              "psmall": 5, "pta": 6, "ptb": 7}

_GUARD_OPS = frozenset((
    "tensor_copy", "tensor_tensor", "tensor_scalar", "tensor_single_scalar",
    "scalar_tensor_tensor", "tensor_add", "tensor_sub", "tensor_mul",
    "tensor_max", "reduce_sum", "reduce_max", "tensor_reduce", "reciprocal",
    "activation", "bn_stats", "bn_aggr"))


class Guarded:

    def __init__(self, eng, sem, nopfn):
        self._e = eng
        self._sem = sem
        self._n = [0]
        self._win = []
        self._nop = nopfn

    def __getattr__(self, name):
        fn = getattr(self._e, name)
        if name not in _GUARD_OPS:
            return fn

        def wrap(*a, **k):
            banks = set()
            for key, v in k.items():
                if key in ("out", "accum_out"):
                    continue
                t = getattr(v, "tensor", None)
                nm = getattr(t, "name", None)
                if nm in PSUM_BANKS:
                    banks.add(PSUM_BANKS[nm])
            if banks:
                if self._win and not banks.issubset(set(self._win)):
                    self._nop(self._e).then_inc(self._sem, 1)
                    self._n[0] += 1
                    self._e.wait_ge(self._sem, self._n[0])
                    self._win = sorted(banks)
                else:
                    w = [b for b in self._win if b not in banks] + sorted(banks)
                    self._win = w[-2:]
            return fn(*a, **k)
        return wrap


class Sem:
    def __init__(self, nc, ctx, name):
        self.h = ctx.enter_context(nc.semaphore(name))
        self.n = 0

    def bump(self, k):
        self.n += k
        return self.n


class Arena:
    def __init__(self, nc, ctx, bytes_per_partition):
        self.cap = bytes_per_partition
        self.t = ctx.enter_context(
            nc.sbuf_tensor("arena", [P, bytes_per_partition], mybir.dt.uint8))
        self.off = 0

    def reset(self):
        self.off = 0

    def take(self, cols, dtype):
        size = {F32: 4, BF16: 2, I32: 4}[dtype]
        self.off = (self.off + 3) & ~3
        start = self.off
        self.off += cols * size
        assert self.off <= self.cap, f"arena overflow {self.off} > {self.cap}"
        return self.t[:, start:start + cols * size].bitcast(dtype)


def build(debug_taps=(), stop_after=None, max_events=None):
    nc = bass.Bass("TRN2", target_bir_lowering=False,
                   detect_race_conditions=False)
    ap = lambda name, shp, dt: nc.dram_tensor(name, shp, dt,
                                              kind="ExternalInput").ap()
    x_in = ap("x_in", [S_FULL, D], F32)
    wqT = ap("wqT", [D, D], F32)
    wkT = ap("wkT", [D, D], F32)
    wvT = ap("wvT", [D, D], F32)
    woT = ap("woT", [D, D], F32)
    mems = ap("mems", [M, H, HD, HD], F32)
    mnT = ap("mnT", [HD, H * M], F32)
    mn2 = ap("mn2", [1, M * D], F32)
    gate_row = ap("gate_row", [1, H], F32)
    wgT = ap("wgT", [D, HID], I32)
    wuT = ap("wuT", [D, HID], I32)
    wdT = ap("wdT", [HID, D], I32)
    sg_t = ap("sg_t", [P, NH], F32)
    su_t = ap("su_t", [P, NH], F32)
    sd_t = ap("sd_t", [P, ND], F32)
    ident_in = ap("ident_bf", [P, P], BF16)
    ones_bf_in = ap("ones_bf", [P, P], BF16)
    ones_f_in = ap("ones_f_in", [1, P], F32)
    maskT_in = ap("maskT", [P, P], F32)
    mask_col_in = ap("mask_col", [P, 16], F32)
    out = nc.dram_tensor("out", [S_OWN, D], F32, kind="ExternalOutput").ap()

    dscr = lambda name, shp: nc.dram_tensor(name, shp, BF16,
                                            kind="Internal").ap()
    S_xnt = dscr("S_xnt", [NT, P, D])
    S_qp = dscr("S_qp", [H, P, S_OWN])
    S_kp = dscr("S_kp", [H, P, S_FULL])
    S_vc = dscr("S_vc", [CH, P, D])
    S_comb = dscr("S_comb", [H, P, S_OWN])
    S_attn = dscr("S_attn", [ND, P, S_OWN])
    S_h = dscr("S_h", [NH, P, S_OWN])
    S_wd = dscr("S_wd", [ND, P, HID])
    S_sum = S_qp

    taps = {}
    tap_shapes = {"xn": ([NT, P, D], BF16), "qp": ([H, P, S_OWN], BF16),
                  "kp": ([H, P, S_FULL], BF16), "vc": ([CH, P, D], BF16),
                  "comb": ([H, P, S_OWN], BF16), "attn": ([ND, P, S_OWN], BF16),
                  "hfm": ([NH, P, S_OWN], BF16), "srun": ([P, H * 132], F32),
                  "wrow": ([1, 64], F32), "qbar": ([P, H], F32),
                  "xnbar": ([1, D], F32)}
    for name in debug_taps:
        shp, dt = tap_shapes[name]
        taps[name] = nc.dram_tensor("tap_" + name, shp, dt,
                                    kind="ExternalOutput").ap()

    with ExitStack() as ctx:
        e = ctx.enter_context
        ident = e(nc.sbuf_tensor("ident", [P, P], BF16))
        ones_b = e(nc.sbuf_tensor("ones_b", [P, P], BF16))
        ones_f = e(nc.sbuf_tensor("ones_f", [1, P], F32))
        maskT = e(nc.sbuf_tensor("maskTt", [P, P], F32))
        mask_c = e(nc.sbuf_tensor("mask_c", [P, 16], F32))
        mnT_f = e(nc.sbuf_tensor("mnT_f", [HD, H * M], F32))
        mnT_b = e(nc.sbuf_tensor("mnT_b", [HD, H * M], BF16))
        gaterow = e(nc.sbuf_tensor("gaterow", [1, H], F32))
        sg_s = e(nc.sbuf_tensor("sg_s", [P, NH], F32))
        su_s = e(nc.sbuf_tensor("su_s", [P, NH], F32))
        sd_s = e(nc.sbuf_tensor("sd_s", [P, ND], F32))
        xnbar_r = e(nc.sbuf_tensor("xnbar_r", [1, D], F32))
        xnbar_rb = e(nc.sbuf_tensor("xnbar_rb", [1, D], BF16))
        xnbar_f = e(nc.sbuf_tensor("xnbar_f", [P, ND], BF16))
        qbar_f = e(nc.sbuf_tensor("qbar_f", [P, H], F32))
        wg_bc = e(nc.sbuf_tensor("wg_bc", [P, H * M], F32))
        omg_bc = e(nc.sbuf_tensor("omg_bc", [P, H], F32))
        smalls = e(nc.sbuf_tensor("smalls", [1, 1024], F32))

        arena = Arena(nc, ctx, 180 * 1024)

        PB = [e(nc.psum_tensor(f"pb{i}", [P, 512], F32)) for i in range(5)]
        PS = e(nc.psum_tensor("psmall", [P, 512], F32))
        PTA = e(nc.psum_tensor("pta", [P, 512], BF16))
        PTB = e(nc.psum_tensor("ptb", [P, 512], BF16))
        PT2 = [PTA, PTB]

        sched = []

        def ev(eng, fn):
            sched.append((eng, fn))

        sems = {}
        ALIAS = {
            "lc": "qS0", "lx0": "qS1", "lx1": "qS2", "lmn20": "qS3",
            "lmn21": "qS4", "lxr": "qS5", "lwp0": "qS6", "lwp1": "qS7",
            "lwv": "qS8", "lkt0": "qS9", "lkt1": "qS3", "lkp0": "qS1",
            "lkp1": "qS2", "lqp0": "qS4", "lqp1": "qS5", "lvc0": "qS6",
            "lvc1": "qS7", "lmem0": "qS8", "lmem1": "qS9", "lcb": "qS0",
            "lwo0": "qS3", "lwo1": "qS4", "lx5": "qS0", "lg50": "qS1",
            "lg51": "qS2", "lu50": "qS3", "lu51": "qS4", "ld60": "qS5",
            "ld61": "qS6", "l6h": "qS7", "l6w0": "qS8", "l6w1": "qS9",
            "l6a0": "qS1", "l6a1": "qS2", "l7s0": "qS3", "l7s1": "qS4",
            "l7x0": "qS5", "l7x1": "qS6",
            "gSp": "qG0", "gqp": "qG1", "gvc": "qG2", "gCB": "qG3",
            "gat": "qG0", "g5": "qG1", "g6p": "qG2", "g6": "qG0",
            "g7": "qG1", "gtap": "qG2",
        }

        def sem(name):
            name = ALIAS.get(name, name)
            if name not in sems:
                sems[name] = Sem(nc, ctx, name)
            return sems[name]

        def dma(eng, dst, src, s, transpose=False):
            v = s.bump(16)
            if transpose:
                ev(eng, lambda E, dst=dst, src=src, s=s: E.dma_start(
                    out=dst, in_=src, transpose=True).then_inc(s.h, 16))
            else:
                ev(eng, lambda E, dst=dst, src=src, s=s: E.dma_start(
                    out=dst, in_=src).then_inc(s.h, 16))
            return v

        def wait(eng, s, v):
            assert v <= s.n, f"wait target {v} beyond planned {s.n} ({eng})"
            ev(eng, lambda E, s=s, v=v: E.wait_ge(s.h, v))

        vrt = sem("vrt")
        art = sem("art")
        _rtc = [0, 0]

        def rt(E):
            _rtc[0] += 1
            E.engine_nop().then_inc(vrt.h, 1)
            E.wait_ge(vrt.h, _rtc[0])

        def rt_s(E):
            _rtc[1] += 1
            E.activation(out=smalls[0:1, 1020:1021], in_=smalls[0:1, 1016:1017],
                         func=AF.Copy, bias=0.0,
                         scale=1.0).then_inc(art.h, 1)
            E.wait_ge(art.h, _rtc[1])

        lc = sem("lc")
        for dst, src in [(ident[:], ident_in[:]), (ones_b[:], ones_bf_in[:]),
                         (ones_f[:], ones_f_in[:]), (maskT[:], maskT_in[:]),
                         (mask_c[:], mask_col_in[:]), (mnT_f[:], mnT[:]),
                         (gaterow[:], gate_row[:]), (sg_s[:], sg_t[:]),
                         (su_s[:], su_t[:]), (sd_s[:], sd_t[:])]:
            dma("sync", dst, src, lc)
        LC_ALL = lc.n

        def init_smalls(E):
            E.memset(smalls[:], 0.0)
            rt(E)
        ev("vector", init_smalls)
        wait("vector", lc, LC_ALL)
        ev("vector", lambda E: E.tensor_copy(out=mnT_b[:], in_=mnT_f[:]))

        arena.reset()
        mn2st = arena.take(2 * D, F32)
        xst = arena.take(2 * D, F32)
        sqs = arena.take(2 * D, F32)
        xnt = arena.take(2 * D, BF16)
        cols = arena.take(64, F32)
        lx = [sem("lx0"), sem("lx1")]
        aS = sem("aS")
        vR = sem("vR")
        vStat = sem("vStat")
        aRstd = sem("aRstd")
        vXn = sem("vXn")
        tTr = sem("tTr")
        gSp = sem("gSp")
        lmn2 = [sem("lmn20"), sem("lmn21")]
        vmn2 = sem("vmn2")
        for m_i in range(M):
            msl = m_i % 2
            if m_i >= 2:
                wait("sync", vmn2, m_i - 1)
            vld = dma("sync", mn2st[0:1, msl * D:msl * D + D],
                      mn2[:, m_i * D:(m_i + 1) * D], lmn2[msl])
            wait("vector", lmn2[msl], vld)
            def mred(E, m_i=m_i, msl=msl):
                E.reduce_sum(out=smalls[:, m_i:m_i + 1],
                             in_=mn2st[0:1, msl * D:msl * D + D],
                             axis=AX.X).then_inc(vmn2.h, 1)
            ev("vector", mred)
            vmn2.n += 1
        wait("vector", vmn2, vmn2.n)
        ev("vector", lambda E: E.tensor_scalar(
            out=smalls[:, 4:8], in0=smalls[:, 0:4], scalar1=1e-6,
            scalar2=None, op0=ALU.is_ge))
        ev("vector", lambda E: E.tensor_scalar(
            out=smalls[:, 8:12], in0=smalls[:, 4:8], scalar1=1.0,
            scalar2=-BIG, op0=ALU.subtract, op1=ALU.mult))
        for T in range(NT):
            sl = T % 2
            if T >= 2:
                wait("sync", vXn, T - 1)
            vld = dma("sync", xst[:, sl * D:(sl + 1) * D],
                      x_in[T * P:(T + 1) * P, :], lx[sl])
            wait("scalar", lx[sl], vld)
            def sq(E, sl=sl):
                E.activation(out=sqs[:, sl * D:(sl + 1) * D],
                             in_=xst[:, sl * D:(sl + 1) * D], func=AF.Square,
                             bias=0.0, scale=1.0,
                             accum_out=cols[:, 8 * sl:8 * sl + 1]
                             ).then_inc(aS.h, 1)
            ev("scalar", sq)
            aS.n += 1
            wait("vector", lx[sl], vld)
            def red(E, sl=sl):
                E.reduce_sum(out=cols[:, 8 * sl + 1:8 * sl + 2],
                             in_=xst[:, sl * D:(sl + 1) * D],
                             axis=AX.X).then_inc(vR.h, 1)
            ev("vector", red)
            vR.n += 1
            wait("vector", vR, vR.n)
            wait("vector", aS, aS.n)
            def stats(E, sl=sl):
                c0 = 8 * sl
                E.tensor_scalar(out=cols[:, c0 + 2:c0 + 3],
                                in0=cols[:, c0 + 1:c0 + 2], scalar1=1.0 / D,
                                scalar2=None, op0=ALU.mult)
                E.tensor_scalar(out=cols[:, c0 + 3:c0 + 4],
                                in0=cols[:, c0:c0 + 1], scalar1=1.0 / D,
                                scalar2=LN_EPS, op0=ALU.mult, op1=ALU.add)
                rt(E)
                E.tensor_tensor(out=cols[:, c0 + 4:c0 + 5],
                                in0=cols[:, c0 + 2:c0 + 3],
                                in1=cols[:, c0 + 2:c0 + 3], op=ALU.mult)
                rt(E)
                E.tensor_tensor(out=cols[:, c0 + 5:c0 + 6],
                                in0=cols[:, c0 + 3:c0 + 4],
                                in1=cols[:, c0 + 4:c0 + 5],
                                op=ALU.subtract).then_inc(vStat.h, 1)
            ev("vector", stats)
            vStat.n += 1
            wait("scalar", vStat, vStat.n)
            def rstd(E, sl=sl):
                c0 = 8 * sl
                E.activation(out=cols[:, c0 + 6:c0 + 7],
                             in_=cols[:, c0 + 5:c0 + 6], func=AF.Ln,
                             bias=0.0, scale=1.0)
                rt_s(E)
                E.activation(out=cols[:, c0 + 7:c0 + 8],
                             in_=cols[:, c0 + 6:c0 + 7], func=AF.Exp,
                             bias=0.0, scale=-0.5).then_inc(aRstd.h, 1)
            ev("scalar", rstd)
            aRstd.n += 1
            wait("vector", aRstd, aRstd.n)
            if T >= 2:
                wait("vector", tTr, 4 * (T - 1))
                wait("vector", gSp, 16 * (T - 1))
            def xnc(E, sl=sl):
                c0 = 8 * sl
                E.tensor_scalar(out=xnt[:, sl * D:(sl + 1) * D],
                                in0=xst[:, sl * D:(sl + 1) * D],
                                scalar1=cols[:, c0 + 2:c0 + 3],
                                scalar2=cols[:, c0 + 7:c0 + 8],
                                op0=ALU.subtract, op1=ALU.mult
                                ).then_inc(vXn.h, 1)
            ev("vector", xnc)
            vXn.n += 1
            wait("tensor", vXn, vXn.n)
            for g4 in range(4):
                def xbm(E, sl=sl, g4=g4, T=T):
                    E.matmul(PB[g4][0:1, 0:512], ones_b[:, 0:1],
                             xnt[:, sl * D + g4 * 512:sl * D + (g4 + 1) * 512],
                             start=(T == 0), stop=(T == NT - 1)
                             ).then_inc(tTr.h, 1)
                ev("tensor", xbm)
                tTr.n += 1
            wait("gpsimd", vXn, vXn.n)
            dma("gpsimd",
                bass.AP(S_xnt.tensor, T * P * D, [[D, P], [1, D]]),
                xnt[:, sl * D:(sl + 1) * D], gSp)
        vXb = sem("vXb")
        wait("vector", tTr, tTr.n)
        def xb_ev(E):
            for g4 in range(4):
                E.tensor_scalar(out=xnbar_r[:, g4 * 512:(g4 + 1) * 512],
                                in0=PB[g4][0:1, 0:512], scalar1=1.0 / S_FULL,
                                scalar2=None, op0=ALU.mult)
            rt(E)
            E.tensor_copy(out=xnbar_rb[:], in_=xnbar_r[:]).then_inc(vXb.h, 1)
        ev("vector", xb_ev)
        vXb.n += 1
        tXb = sem("tXb")
        wait("tensor", vXb, vXb.n)
        def xb_mm(E):
            for Dc in range(ND):
                mm = E.matmul(PS[:, Dc:Dc + 1],
                              xnbar_rb[:, Dc * P:(Dc + 1) * P],
                              ones_b[0:1, 0:1], start=True, stop=True)
            mm.then_inc(tXb.h, 1)
        ev("tensor", xb_mm)
        tXb.n += 1
        vXb2 = sem("vXb2")
        wait("vector", tXb, tXb.n)
        ev("vector", lambda E: E.tensor_copy(
            out=xnbar_f[:], in_=PS[:, 0:ND]).then_inc(vXb2.h, 1))
        vXb2.n += 1
        if "xnbar" in taps:
            gt = sem("gtap")
            wait("gpsimd", vXb, vXb.n)
            dma("gpsimd", taps["xnbar"][:], xnbar_r[:], gt)
        if "xn" in taps:
            gt = sem("gtap")
            wait("gpsimd", gSp, gSp.n)
            for T in range(NT):
                dma("gpsimd",
                    bass.AP(taps["xn"].tensor, T * P * D, [[D, P], [1, D]]),
                    bass.AP(S_xnt.tensor, T * P * D, [[D, P], [1, D]]), gt)

        if stop_after == "ph1":
            wait("gpsimd", gSp, gSp.n)
            return _dispatch(nc, e, sched, max_events)

        arena.reset()
        xn_res = arena.take(ND * D, BF16)
        wv_res = arena.take(ND * D, BF16)
        wstF = arena.take(2 * D, F32)
        wstB = arena.take(2 * D, BF16)
        elu_t = arena.take(2 * 512, F32)
        elu_t2 = arena.take(2 * 512, F32)
        prj_o = arena.take(2 * 512, BF16)

        lxr = sem("lxr")
        lwp = [sem("lwp0"), sem("lwp1")]
        wcast = sem("wcast")
        mmP = sem("mmP")
        velu = sem("velu")
        aexp = sem("aexp")
        gqp = sem("gqp")
        mmV = sem("mmV")
        vev2 = sem("vev2")
        gvc = sem("gvc")
        mm_marks = []
        velu_marks = []
        aexp_marks = []

        def xn_resident_load(half):
            for Dc in range(ND):
                src = bass.AP(S_xnt.tensor,
                              half * CHO * P * D + Dc * P,
                              [[D, S_OWN], [1, P]])
                dma("sync", xn_res[:, Dc * D:(Dc + 1) * D], src, lxr,
                    transpose=True)
            return lxr.n

        def fm_projection(wT_ap, s_out, out_width, elu, qbar_too, xn_ready_v,
                          out_col_off):
            if gvc.n > 0:
                wait("vector", gvc, gvc.n)
            for E_i in range(ND):
                sl = E_i % 2
                gE = len(mm_marks)
                if gE >= 2:
                    wait("sync", wcast, wcast.n - 1)
                src = bass.AP(wT_ap.tensor, E_i * P,
                              [[D, P], [P * D, ND], [1, P]])
                vld = dma("sync", wstF[:, sl * D:(sl + 1) * D], src, lwp[sl])
                wait("vector", lwp[sl], vld)
                if gE >= 2:
                    wait("vector", mmP, mm_marks[gE - 2])
                def wc(E, sl=sl):
                    E.tensor_copy(out=wstB[:, sl * D:(sl + 1) * D],
                                  in_=wstF[:, sl * D:(sl + 1) * D]
                                  ).then_inc(wcast.h, 1)
                ev("vector", wc)
                wcast.n += 1
                if E_i == 0:
                    wait("tensor", lxr, xn_ready_v)
                wait("tensor", wcast, wcast.n)
                if qbar_too and E_i == 0:
                    wait("tensor", vXb2, vXb2.n)
                for g in range(4):
                    bank = PB[g % 4]
                    gk = len(velu_marks)
                    if gk >= 4:
                        wait("tensor", velu, velu_marks[gk - 4])
                    def mmq(E, sl=sl, g=g, bank=bank):
                        for Dc in range(ND):
                            mm = E.matmul(
                                bank[:, 0:512],
                                wstB[:, sl * D + Dc * P:sl * D + (Dc + 1) * P],
                                xn_res[:, Dc * D + g * 512:Dc * D + (g + 1) * 512],
                                start=(Dc == 0), stop=(Dc == ND - 1))
                        mm.then_inc(mmP.h, 1)
                    ev("tensor", mmq)
                    mmP.n += 1
                    osl = gk % 2
                    if elu:
                        wait("vector", mmP, mmP.n)
                        if gk >= 2:
                            wait("vector", aexp, aexp_marks[gk - 2])
                        def e1(E, bank=bank, osl=osl):
                            E.tensor_scalar(
                                out=elu_t[:, osl * 512:(osl + 1) * 512],
                                in0=bank[:, 0:512], scalar1=0.0, scalar2=None,
                                op0=ALU.min).then_inc(velu.h, 1)
                        ev("vector", e1)
                        velu.n += 1
                        wait("scalar", velu, velu.n)
                        def e2(E, osl=osl):
                            E.activation(
                                out=elu_t2[:, osl * 512:(osl + 1) * 512],
                                in_=elu_t[:, osl * 512:(osl + 1) * 512],
                                func=AF.Exp, bias=0.0, scale=1.0
                                ).then_inc(aexp.h, 1)
                        ev("scalar", e2)
                        aexp.n += 1
                        aexp_marks.append(aexp.n)
                        wait("vector", aexp, aexp.n)
                        if gqp.n >= 32:
                            wait("vector", gqp, gqp.n - 16)
                        def e3(E, bank=bank, osl=osl):
                            E.scalar_tensor_tensor(
                                out=prj_o[:, osl * 512:(osl + 1) * 512],
                                in0=bank[:, 0:512], scalar=0.0,
                                in1=elu_t2[:, osl * 512:(osl + 1) * 512],
                                op0=ALU.max, op1=ALU.add).then_inc(velu.h, 1)
                        ev("vector", e3)
                        velu.n += 1
                    else:
                        wait("vector", mmP, mmP.n)
                        aexp_marks.append(aexp.n)
                        if gqp.n >= 32:
                            wait("vector", gqp, gqp.n - 16)
                        def e3c(E, bank=bank, osl=osl):
                            E.tensor_copy(
                                out=prj_o[:, osl * 512:(osl + 1) * 512],
                                in_=bank[:, 0:512]).then_inc(velu.h, 1)
                        ev("vector", e3c)
                        velu.n += 1
                    velu_marks.append(velu.n)
                    wait("gpsimd", velu, velu.n)
                    dst = bass.AP(s_out.tensor,
                                  E_i * P * out_width + out_col_off + g * 512,
                                  [[out_width, P], [1, 512]])
                    dma("gpsimd", dst, prj_o[:, osl * 512:(osl + 1) * 512],
                        gqp)
                if qbar_too:
                    def mqb(E, sl=sl, E_i=E_i):
                        for Dc in range(ND):
                            mm = E.matmul(
                                PS[:, E_i:E_i + 1],
                                wstB[:, sl * D + Dc * P:sl * D + (Dc + 1) * P],
                                xnbar_f[:, Dc:Dc + 1],
                                start=(Dc == 0), stop=(Dc == ND - 1))
                        mm.then_inc(mmP.h, 1)
                    ev("tensor", mqb)
                    mmP.n += 1
                mm_marks.append(mmP.n)

        def v_projection(half, lwv_ready):
            wait("vector", gqp, gqp.n)
            base_t = half * CHO
            for t_loc in range(CHO):
                t = base_t + t_loc
                for eg in range(4):
                    bank = PB[eg % 2]
                    k = mmV.n
                    if k >= 2:
                        wait("tensor", vev2, k - 1)
                    if t_loc == 0 and eg == 0:
                        wait("tensor", lwv, lwv_ready)
                    def mmv(E, t_loc=t_loc, eg=eg, bank=bank):
                        for Dc in range(ND):
                            mm = E.matmul(
                                bank[:, 0:512],
                                xn_res[:, Dc * D + t_loc * P:Dc * D + (t_loc + 1) * P],
                                wv_res[:, Dc * D + eg * 512:Dc * D + (eg + 1) * 512],
                                start=(Dc == 0), stop=(Dc == ND - 1))
                        mm.then_inc(mmV.h, 1)
                    ev("tensor", mmv)
                    mmV.n += 1
                    osl = (mmV.n - 1) % 2
                    wait("vector", mmV, mmV.n)
                    if gvc.n >= 32:
                        wait("vector", gvc, gvc.n - 16)
                    def vev(E, bank=bank, osl=osl):
                        E.tensor_copy(
                            out=prj_o[:, osl * 512:(osl + 1) * 512],
                            in_=bank[:, 0:512]).then_inc(vev2.h, 1)
                    ev("vector", vev)
                    vev2.n += 1
                    wait("gpsimd", vev2, vev2.n)
                    dma("gpsimd",
                        bass.AP(S_vc.tensor, t * P * D + eg * 512,
                                [[D, P], [1, 512]]),
                        prj_o[:, osl * 512:(osl + 1) * 512], gvc)

        lwv = sem("lwv")
        wvcast = sem("wvcast")

        def load_wv():
            for Dc in range(ND):
                sl = Dc % 2
                if Dc >= 2:
                    wait("sync", wvcast, Dc - 1)
                vld = dma("sync", wstF[:, sl * D:(sl + 1) * D],
                          wvT[Dc * P:(Dc + 1) * P, :], lwv)
                wait("vector", lwv, vld)
                def wvc(E, sl=sl, Dc=Dc):
                    E.tensor_copy(out=wv_res[:, Dc * D:(Dc + 1) * D],
                                  in_=wstF[:, sl * D:(sl + 1) * D]
                                  ).then_inc(wvcast.h, 1)
                ev("vector", wvc)
                wvcast.n += 1
            return lwv.n

        wait("sync", gSp, gSp.n)
        v_xn_own = xn_resident_load(1)
        fm_projection(wqT, S_qp, S_OWN, True, True, v_xn_own, 0)
        vqb = sem("vqb")
        wait("vector", mmP, mmP.n)
        ev("vector", lambda E: E.tensor_copy(
            out=qbar_f[:], in_=PS[:, 0:H]).then_inc(vqb.h, 1))
        vqb.n += 1
        if "qbar" in taps:
            gt = sem("gtap")
            wait("gpsimd", vqb, vqb.n)
            dma("gpsimd", taps["qbar"][:], qbar_f[:], gt)

        trel = sem("trel")
        wait("tensor", vqb, vqb.n)
        def relmm(E):
            for h in range(H):
                mm = E.matmul(PS[0:1, 64 + 4 * h:64 + 4 * h + 4],
                              qbar_f[:, h:h + 1], mnT_f[:, 4 * h:4 * h + 4],
                              start=True, stop=True)
            mm.then_inc(trel.h, 1)
        ev("tensor", relmm)
        trel.n += 1
        aw = sem("aw")
        vw = sem("vw")
        wait("scalar", trel, trel.n)
        ev("scalar", lambda E: E.activation(
            out=smalls[:, 16:80], in_=PS[0:1, 64:128], func=AF.Copy,
            bias=0.0, scale=SCALE).then_inc(aw.h, 1))
        aw.n += 1
        ev("scalar", lambda E: E.activation(
            out=smalls[:, 256:272], in_=gaterow[:], func=AF.Sigmoid,
            bias=0.0, scale=1.0).then_inc(aw.h, 1))
        aw.n += 1
        wait("vector", aw, aw.n)
        def wchain1(E):
            E.tensor_tensor(out=smalls[:, 16:80], in0=smalls[:, 16:80],
                            in1=bass.AP(smalls, 8, [[1024, 1], [0, H], [1, M]]),
                            op=ALU.subtract)
            rt(E)
            E.reduce_max(out=smalls[:, 80:96],
                         in_=bass.AP(smalls, 16, [[1024, 1], [4, H], [1, 4]]),
                         axis=AX.X).then_inc(vw.h, 1)
        ev("vector", wchain1)
        vw.n += 1
        wait("vector", vw, vw.n)
        def wchain2(E):
            E.tensor_tensor(out=smalls[:, 96:160], in0=smalls[:, 16:80],
                            in1=bass.AP(smalls, 80, [[1024, 1], [1, H], [0, 4]]),
                            op=ALU.subtract).then_inc(vw.h, 1)
        ev("vector", wchain2)
        vw.n += 1
        wait("scalar", vw, vw.n)
        ev("scalar", lambda E: E.activation(
            out=smalls[:, 96:160], in_=smalls[:, 96:160], func=AF.Exp,
            bias=0.0, scale=1.0).then_inc(aw.h, 1))
        aw.n += 1
        wait("vector", aw, aw.n)
        def wchain3(E):
            E.tensor_tensor(out=smalls[:, 96:160], in0=smalls[:, 96:160],
                            in1=bass.AP(smalls, 4, [[1024, 1], [0, H], [1, M]]),
                            op=ALU.mult)
            rt(E)
            E.reduce_sum(out=smalls[:, 160:176],
                         in_=bass.AP(smalls, 96, [[1024, 1], [4, H], [1, 4]]),
                         axis=AX.X).then_inc(vw.h, 1)
        ev("vector", wchain3)
        vw.n += 1
        wait("vector", vw, vw.n)
        def wchain4(E):
            E.tensor_scalar(out=smalls[:, 160:176], in0=smalls[:, 160:176],
                            scalar1=1e-30, scalar2=None, op0=ALU.max)
            rt(E)
            E.reciprocal(out=smalls[:, 176:192],
                         in_=smalls[:, 160:176]).then_inc(vw.h, 1)
        ev("vector", wchain4)
        vw.n += 1
        wait("vector", vw, vw.n)
        def wchain5(E):
            E.tensor_tensor(out=smalls[:, 192:256], in0=smalls[:, 96:160],
                            in1=bass.AP(smalls, 176, [[1024, 1], [1, H], [0, 4]]),
                            op=ALU.mult)
            E.tensor_scalar(out=smalls[:, 272:288], in0=smalls[:, 256:272],
                            scalar1=1.0, scalar2=-1.0, op0=ALU.subtract,
                            op1=ALU.mult)
            rt(E)
            E.tensor_tensor(out=smalls[:, 288:352], in0=smalls[:, 192:256],
                            in1=bass.AP(smalls, 256, [[1024, 1], [1, H], [0, 4]]),
                            op=ALU.mult).then_inc(vw.h, 1)
        ev("vector", wchain5)
        vw.n += 1
        if "wrow" in taps:
            gt = sem("gtap")
            wait("gpsimd", vw, vw.n)
            dma("gpsimd", taps["wrow"][:], smalls[:, 192:256], gt)
        tbc = sem("tbc")
        wait("tensor", vw, vw.n)
        def bcmm(E):
            E.matmul(PB[4][:, 0:64], ones_f[:, :], smalls[:, 288:352],
                     start=True, stop=True)
            E.matmul(PB[4][:, 64:80], ones_f[:, :], smalls[:, 272:288],
                     start=True, stop=True).then_inc(tbc.h, 1)
        ev("tensor", bcmm)
        tbc.n += 1
        vbc = sem("vbc")
        wait("vector", tbc, tbc.n)
        def bcev(E):
            E.tensor_copy(out=wg_bc[:], in_=PB[4][:, 0:64])
            E.tensor_copy(out=omg_bc[:], in_=PB[4][:, 64:80]).then_inc(vbc.h, 1)
        ev("vector", bcev)
        vbc.n += 1

        fm_projection(wkT, S_kp, S_FULL, True, False, v_xn_own, S_OWN)
        v_wv = load_wv()
        v_projection(1, v_wv)
        wait("sync", mmP, mm_marks[-1])
        wait("sync", mmV, mmV.n)
        v_xn_pre = xn_resident_load(0)
        fm_projection(wkT, S_kp, S_FULL, True, False, v_xn_pre, 0)
        v_projection(0, v_wv)
        for name, scr, n_, wid in (("qp", S_qp, H, S_OWN),
                                   ("kp", S_kp, H, S_FULL),
                                   ("vc", S_vc, CH, D)):
            if name in taps:
                gt = sem("gtap")
                wait("gpsimd", gvc, gvc.n)
                wait("gpsimd", gqp, gqp.n)
                for i in range(n_):
                    dma("gpsimd",
                        bass.AP(taps[name].tensor, i * P * wid,
                                [[wid, P], [1, wid]]),
                        bass.AP(scr.tensor, i * P * wid, [[wid, P], [1, wid]]),
                        gt)

        if stop_after == "ph2":
            wait("gpsimd", gvc, gvc.n)
            wait("gpsimd", gqp, gqp.n)
            return _dispatch(nc, e, sched, max_events)

        arena.reset()
        kp_c = arena.take(2 * D, BF16)
        kptm = arena.take(2 * D, BF16)
        qp_c = arena.take(2 * D, BF16)
        vc_c = arena.take(2 * D, BF16)
        cstg = arena.take(2 * D, BF16)
        scT_b = arena.take(2 * P, BF16)
        dencol = arena.take(32, F32)
        combf = arena.take(2 * P, F32)
        srun = [arena.take(H * 132, F32) for _ in range(2)]
        sbf = arena.take(H * 132, BF16)
        mem_b = arena.take(H * M * HD, BF16)
        memf = arena.take(2 * D, F32)

        lkp = [sem("lkp0"), sem("lkp1")]
        lkt = [sem("lkt0"), sem("lkt1")]
        lqp = [sem("lqp0"), sem("lqp1")]
        lvc = [sem("lvc0"), sem("lvc1")]
        tSC = sem("tSC")
        vMK = sem("vMK")
        tNUM = sem("tNUM")
        tST = sem("tST")
        vUP = sem("vUP")
        aSB = sem("aSB")
        vCB = sem("vCB")
        aCB = sem("aCB")
        tCT = sem("tCT")
        vCE = sem("vCE")
        gCB = sem("gCB")
        vDN = sem("vDN")
        wait("sync", mmP, mm_marks[-1])
        wait("sync", mmV, mmV.n)
        wait("sync", gvc, gvc.n)
        wait("sync", gqp, gqp.n)
        lmem = [sem("lmem0"), sem("lmem1")]
        mc = sem("mc")
        zst = sem("zst")
        for m_i in range(M):
            msl = m_i % 2
            if m_i >= 2:
                wait("sync", mc, m_i - 1)
            src = bass.AP(mems.tensor, m_i * H * HD * HD,
                          [[HD, P], [HD * HD, H], [1, HD]])
            v = dma("sync", memf[:, msl * D:(msl + 1) * D], src, lmem[msl])
            wait("vector", lmem[msl], v)
            memv = mem_b.rearrange("p (h m e) -> p h m e", h=H, m=M, e=HD)
            def cast_mem(E, m_i=m_i, msl=msl, memv=memv):
                E.tensor_copy(
                    out=memv[:, :, m_i:m_i + 1, :],
                    in_=memf[:, msl * D:(msl + 1) * D]).then_inc(mc.h, 1)
            ev("vector", cast_mem)
            mc.n += 1
        ev("vector", lambda E: E.memset(srun[0][:], 0.0).then_inc(zst.h, 1))
        ev("vector", lambda E: E.memset(sbf[:], 0.0).then_inc(zst.h, 1))
        zst.n += 2

        state_idx = 0
        own_j = 0
        for c in range(CH):
            own = c >= CHO
            sl = c % 2
            if c >= 2:
                wait("sync", tST, 16 * (c - 1))
            dma("sync", kp_c[:, sl * D:(sl + 1) * D],
                bass.AP(S_kp.tensor, c * P,
                        [[S_FULL, P], [P * S_FULL, H], [1, P]]), lkp[sl])
            dma("sync", kptm[:, sl * D:(sl + 1) * D],
                bass.AP(S_kp.tensor, c * P, [[S_FULL, H * HD], [1, P]]),
                lkt[sl], transpose=True)
            dma("sync", vc_c[:, sl * D:(sl + 1) * D],
                bass.AP(S_vc.tensor, c * P * D, [[D, P], [1, D]]), lvc[sl])
            if own:
                dma("sync", qp_c[:, sl * D:(sl + 1) * D],
                    bass.AP(S_qp.tensor, (c - CHO) * P,
                            [[S_OWN, P], [P * S_OWN, H], [1, P]]), lqp[sl])
            vk, vkt, vv_ = lkp[sl].n, lkt[sl].n, lvc[sl].n
            vq = lqp[sl].n if own else 0
            wait("tensor", lkp[sl], vk)
            wait("tensor", lkt[sl], vkt)
            wait("tensor", lvc[sl], vv_)
            if own:
                wait("tensor", lqp[sl], vq)
            if c == 0:
                wait("tensor", zst, zst.n)
                wait("tensor", mc, mc.n)
                wait("tensor", vbc, vbc.n)
            if own and c >= CHO + 1:
                wait("vector", gCB, 16 * (c - CHO - 1))
            for h in range(H):
                hsl = h % 2
                koff = sl * D + h * P
                if own:
                    qoff = sl * D + h * P
                    if own_j >= 1:
                        wait("tensor", vMK, vMK.n)
                    def t_sc(E, koff=koff, qoff=qoff):
                        E.matmul(PB[0][:, 128:256],
                                 kp_c[:, koff:koff + P],
                                 qp_c[:, qoff:qoff + P],
                                 start=True, stop=True).then_inc(tSC.h, 1)
                    ev("tensor", t_sc)
                    tSC.n += 1
                    wait("vector", tSC, tSC.n)
                    def v_mk(E, hsl=hsl):
                        E.tensor_tensor(out=scT_b[:, hsl * P:(hsl + 1) * P],
                                        in0=PB[0][:, 128:256], in1=maskT[:],
                                        op=ALU.mult).then_inc(vMK.h, 1)
                    ev("vector", v_mk)
                    vMK.n += 1
                    wait("tensor", vMK, vMK.n)
                    if own_j >= 1:
                        wait("tensor", vCB, vCB.n)
                    def t_num(E, h=h, hsl=hsl, qoff=qoff, koff=koff):
                        E.matmul(PB[1][:, 0:128],
                                 scT_b[:, hsl * P:(hsl + 1) * P],
                                 vc_c[:, koff:koff + P],
                                 start=True, stop=False)
                        E.matmul(PB[1][:, 0:128], qp_c[:, qoff:qoff + P],
                                 sbf[:, h * 132:h * 132 + 128],
                                 start=False, stop=True)
                        E.matmul(PS[:, h:h + 1],
                                 scT_b[:, hsl * P:(hsl + 1) * P],
                                 ones_b[:, 0:1], start=True, stop=False)
                        E.matmul(PS[:, h:h + 1], qp_c[:, qoff:qoff + P],
                                 sbf[:, h * 132 + 128:h * 132 + 129],
                                 start=False, stop=True)
                        E.matmul(PB[2][:, 0:512], qp_c[:, qoff:qoff + P],
                                 mem_b[:, h * M * HD:(h + 1) * M * HD],
                                 start=True, stop=True)
                        E.matmul(PS[:, 128 + 4 * h:128 + 4 * h + 4],
                                 qp_c[:, qoff:qoff + P],
                                 mnT_b[:, 4 * h:4 * h + 4],
                                 start=True, stop=True).then_inc(tNUM.h, 1)
                    ev("tensor", t_num)
                    tNUM.n += 1
                    wait("vector", tNUM, tNUM.n)
                    def v_dn(E, h=h):
                        E.tensor_scalar(out=dencol[:, 0:1], in0=PS[:, h:h + 1],
                                        scalar1=EPS, scalar2=None, op0=ALU.max)
                        E.tensor_scalar(out=dencol[:, 8:12],
                                        in0=PS[:, 128 + 4 * h:128 + 4 * h + 4],
                                        scalar1=EPS, scalar2=None, op0=ALU.max)
                        rt(E)
                        E.reciprocal(out=dencol[:, 1:2], in_=dencol[:, 0:1])
                        E.reciprocal(out=dencol[:, 12:16],
                                     in_=dencol[:, 8:12]).then_inc(vDN.h, 1)
                    ev("vector", v_dn)
                    vDN.n += 1
                    wait("vector", vDN, vDN.n)
                    if own_j >= 1:
                        wait("vector", aCB, aCB.n)
                    def v_cb(E, h=h):
                        E.tensor_tensor(out=dencol[:, 2:3], in0=dencol[:, 1:2],
                                        in1=omg_bc[:, h:h + 1], op=ALU.mult)
                        E.tensor_tensor(out=dencol[:, 16:20],
                                        in0=dencol[:, 12:16],
                                        in1=wg_bc[:, 4 * h:4 * h + 4],
                                        op=ALU.mult)
                        rt(E)
                        E.tensor_scalar(out=combf[:, 0:128],
                                        in0=PB[1][:, 0:128],
                                        scalar1=dencol[:, 2:3], scalar2=None,
                                        op0=ALU.mult)
                        for m_i in range(3):
                            rt(E)
                            E.scalar_tensor_tensor(
                                out=combf[:, 0:128],
                                in0=PB[2][:, m_i * 128:(m_i + 1) * 128],
                                scalar=dencol[:, 16 + m_i:17 + m_i],
                                in1=combf[:, 0:128],
                                op0=ALU.mult, op1=ALU.add)
                        rt(E)
                        E.scalar_tensor_tensor(
                            out=combf[:, 128:256], in0=PB[2][:, 384:512],
                            scalar=dencol[:, 19:20], in1=combf[:, 0:128],
                            op0=ALU.mult, op1=ALU.add).then_inc(vCB.h, 1)
                    ev("vector", v_cb)
                    vCB.n += 1
                    wait("scalar", vCB, vCB.n)
                    def a_cb(E, hsl=hsl):
                        E.activation(out=scT_b[:, hsl * P:(hsl + 1) * P],
                                     in_=combf[:, 128:256], func=AF.Copy,
                                     bias=0.0, scale=1.0).then_inc(aCB.h, 1)
                    ev("scalar", a_cb)
                    aCB.n += 1
                    wait("tensor", aCB, aCB.n)
                    if own_j >= 2:
                        wait("tensor", vCE, vCE.n - 1)
                    def t_ct(E, hsl=hsl):
                        E.transpose(PT2[hsl][:, 0:P],
                                    scT_b[:, hsl * P:(hsl + 1) * P],
                                    ident[:]).then_inc(tCT.h, 1)
                    ev("tensor", t_ct)
                    tCT.n += 1
                    wait("vector", tCT, tCT.n)
                    def v_ce(E, h=h, sl=sl, hsl=hsl):
                        E.tensor_copy(
                            out=cstg[:, sl * D + h * P:sl * D + (h + 1) * P],
                            in_=PT2[hsl][:, 0:P]).then_inc(vCE.h, 1)
                    ev("vector", v_ce)
                    vCE.n += 1
                    own_j += 1
                if tST.n >= 2:
                    wait("tensor", vUP, vUP.n - 1)
                def t_st(E, hsl=hsl, koff=koff):
                    bank = PB[3 + hsl]
                    E.matmul(bank[:, 0:128], kptm[:, koff:koff + P],
                             vc_c[:, koff:koff + P], start=True, stop=False)
                    E.matmul(bank[:, 128:129], kptm[:, koff:koff + P],
                             ones_b[:, 0:1], start=False,
                             stop=True).then_inc(tST.h, 1)
                ev("tensor", t_st)
                tST.n += 1
                wait("vector", tST, tST.n)
                nxt = (state_idx + 1) % 2
                def v_up(E, h=h, hsl=hsl, cur=state_idx, nxt=nxt):
                    bank = PB[3 + hsl]
                    E.tensor_tensor(out=srun[nxt][:, h * 132:h * 132 + 129],
                                    in0=bank[:, 0:129],
                                    in1=srun[cur][:, h * 132:h * 132 + 129],
                                    op=ALU.add).then_inc(vUP.h, 1)
                ev("vector", v_up)
                vUP.n += 1
                wait("scalar", vUP, vUP.n)
                def a_sb(E, h=h, nxt=nxt):
                    E.activation(out=sbf[:, h * 132:h * 132 + 132],
                                 in_=srun[nxt][:, h * 132:h * 132 + 132],
                                 func=AF.Copy, bias=0.0,
                                 scale=1.0).then_inc(aSB.h, 1)
                ev("scalar", a_sb)
                aSB.n += 1
            state_idx = (state_idx + 1) % 2
            if c == CHO - 1:
                wait("vector", aSB, aSB.n)
                nxt = (state_idx + 1) % 2
                def v_mask(E, cur=state_idx, nxt=nxt):
                    for h in range(H):
                        E.tensor_scalar(
                            out=srun[nxt][:, h * 132:h * 132 + 129],
                            in0=srun[cur][:, h * 132:h * 132 + 129],
                            scalar1=mask_c[:, 0:1], scalar2=None,
                            op0=ALU.mult).then_inc(vUP.h, 1)
                ev("vector", v_mask)
                vUP.n += H
                wait("scalar", vUP, vUP.n)
                def a_mask(E, nxt=nxt):
                    for h in range(H):
                        E.activation(out=sbf[:, h * 132:h * 132 + 132],
                                     in_=srun[nxt][:, h * 132:h * 132 + 132],
                                     func=AF.Copy, bias=0.0, scale=1.0
                                     ).then_inc(aSB.h, 1)
                ev("scalar", a_mask)
                aSB.n += H
                state_idx = (state_idx + 1) % 2
            wait("tensor", aSB, aSB.n)
            if own:
                wait("gpsimd", vCE, vCE.n)
                dma("gpsimd",
                    bass.AP(S_comb.tensor, (c - CHO) * P,
                            [[S_OWN, P], [P * S_OWN, H], [1, P]]),
                    cstg[:, sl * D:(sl + 1) * D], gCB)
        if "srun" in taps:
            gt = sem("gtap")
            wait("gpsimd", aSB, aSB.n)
            dma("gpsimd", taps["srun"][:], srun[state_idx][:], gt)
        if "comb" in taps:
            gt = sem("gtap")
            wait("gpsimd", gCB, gCB.n)
            for i in range(H):
                dma("gpsimd",
                    bass.AP(taps["comb"].tensor, i * P * S_OWN,
                            [[S_OWN, P], [1, S_OWN]]),
                    bass.AP(S_comb.tensor, i * P * S_OWN,
                            [[S_OWN, P], [1, S_OWN]]), gt)
        if stop_after == "ph3":
            wait("gpsimd", gCB, gCB.n)
            return _dispatch(nc, e, sched, max_events)

        arena.reset()
        xn_res4 = arena.take(H * D, BF16)
        wstF4 = arena.take(2 * D, F32)
        wstB4 = arena.take(2 * D, BF16)
        prj4 = arena.take(2 * 512, BF16)
        wait("sync", tST, tST.n)
        wait("sync", gCB, gCB.n)
        lcb = sem("lcb")
        for h in range(H):
            dma("sync", xn_res4[:, h * D:h * D + S_OWN],
                bass.AP(S_comb.tensor, h * P * S_OWN,
                        [[S_OWN, P], [1, S_OWN]]), lcb)
        lwo = [sem("lwo0"), sem("lwo1")]
        wocast = sem("wocast")
        mmO = sem("mmO")
        voe = sem("voe")
        gat = sem("gat")
        for E_i in range(ND):
            sl = E_i % 2
            if E_i >= 2:
                wait("sync", wocast, wocast.n - 1)
            src = bass.AP(woT.tensor, E_i * P, [[D, P], [P * D, ND], [1, P]])
            vld = dma("sync", wstF4[:, sl * D:(sl + 1) * D], src, lwo[sl])
            wait("vector", lwo[sl], vld)
            if E_i >= 2:
                wait("vector", mmO, 4 * (E_i - 1))
            def woc(E, sl=sl):
                E.tensor_copy(out=wstB4[:, sl * D:(sl + 1) * D],
                              in_=wstF4[:, sl * D:(sl + 1) * D]
                              ).then_inc(wocast.h, 1)
            ev("vector", woc)
            wocast.n += 1
            if E_i == 0:
                wait("tensor", lcb, lcb.n)
                wait("tensor", vUP, vUP.n)
            wait("tensor", wocast, wocast.n)
            for g in range(4):
                bank = PB[g % 4]
                k = E_i * 4 + g
                if k >= 4:
                    wait("tensor", voe, k - 3)
                def mmo(E, sl=sl, g=g, bank=bank):
                    for h in range(H):
                        mm = E.matmul(
                            bank[:, 0:512],
                            wstB4[:, sl * D + h * P:sl * D + (h + 1) * P],
                            xn_res4[:, h * D + g * 512:h * D + (g + 1) * 512],
                            start=(h == 0), stop=(h == H - 1))
                    mm.then_inc(mmO.h, 1)
                ev("tensor", mmo)
                mmO.n += 1
                osl = k % 2
                wait("vector", mmO, mmO.n)
                if k >= 2:
                    wait("vector", gat, 16 * (k - 1))
                def oev(E, bank=bank, osl=osl):
                    E.tensor_copy(out=prj4[:, osl * 512:(osl + 1) * 512],
                                  in_=bank[:, 0:512]).then_inc(voe.h, 1)
                ev("vector", oev)
                voe.n += 1
                wait("gpsimd", voe, voe.n)
                dma("gpsimd",
                    bass.AP(S_attn.tensor, E_i * P * S_OWN + g * 512,
                            [[S_OWN, P], [1, 512]]),
                    prj4[:, osl * 512:(osl + 1) * 512], gat)
        if "attn" in taps:
            gt = sem("gtap")
            wait("gpsimd", gat, gat.n)
            for i in range(ND):
                dma("gpsimd",
                    bass.AP(taps["attn"].tensor, i * P * S_OWN,
                            [[S_OWN, P], [1, S_OWN]]),
                    bass.AP(S_attn.tensor, i * P * S_OWN,
                            [[S_OWN, P], [1, S_OWN]]), gt)
        if stop_after == "ph4":
            wait("gpsimd", gat, gat.n)
            return _dispatch(nc, e, sched, max_events)

        arena.reset()
        xn_res5 = arena.take(ND * D, BF16)
        wgi = arena.take(2 * D, I32)
        wui = arena.take(2 * D, I32)
        wgb = arena.take(2 * D, BF16)
        wub = arena.take(2 * D, BF16)
        silu_s = arena.take(2 * 512, F32)
        h_stg = arena.take(2 * 512, BF16)
        wait("sync", mmO, mmO.n)
        wait("sync", gat, gat.n)
        lx5 = sem("lx5")
        for Dc in range(ND):
            src = bass.AP(S_xnt.tensor, CHO * P * D + Dc * P,
                          [[D, S_OWN], [1, P]])
            dma("sync", xn_res5[:, Dc * D:(Dc + 1) * D], src, lx5,
                transpose=True)
        lg5 = [sem("lg50"), sem("lg51")]
        lu5 = [sem("lu50"), sem("lu51")]
        dq5 = sem("dq5")
        mm5 = sem("mm5")
        a5 = sem("a5")
        v5 = sem("v5")
        g5 = sem("g5")
        for hc in range(NH):
            sl = hc % 2
            if hc >= 2:
                wait("sync", dq5, 2 * (hc - 1))
            srcg = bass.AP(wgT.tensor, hc * P,
                           [[HID, P], [P * HID, ND], [1, P]])
            srcu = bass.AP(wuT.tensor, hc * P,
                           [[HID, P], [P * HID, ND], [1, P]])
            vg = dma("sync", wgi[:, sl * D:(sl + 1) * D], srcg, lg5[sl])
            vu = dma("sync", wui[:, sl * D:(sl + 1) * D], srcu, lu5[sl])
            wait("vector", lg5[sl], vg)
            wait("vector", lu5[sl], vu)
            if hc >= 2:
                wait("vector", mm5, 8 * (hc - 1))
            def dqg(E, sl=sl):
                E.tensor_scalar(out=wgb[:, sl * D:(sl + 1) * D],
                                in0=wgi[:, sl * D:(sl + 1) * D],
                                scalar1=-128.0, scalar2=None,
                                op0=ALU.add).then_inc(dq5.h, 1)
                E.tensor_scalar(out=wub[:, sl * D:(sl + 1) * D],
                                in0=wui[:, sl * D:(sl + 1) * D],
                                scalar1=-128.0, scalar2=None,
                                op0=ALU.add).then_inc(dq5.h, 1)
            ev("vector", dqg)
            dq5.n += 2
            if hc == 0:
                wait("tensor", lx5, lx5.n)
            wait("tensor", dq5, dq5.n)
            for g in range(4):
                k = hc * 4 + g
                bg = PB[(g % 2) * 2]
                bu = PB[(g % 2) * 2 + 1]
                if k >= 2:
                    wait("tensor", v5, k - 1)
                def mmgu(E, sl=sl, g=g, bg=bg, bu=bu):
                    for Dc in range(ND):
                        E.matmul(bg[:, 0:512],
                                 wgb[:, sl * D + Dc * P:sl * D + (Dc + 1) * P],
                                 xn_res5[:, Dc * D + g * 512:Dc * D + (g + 1) * 512],
                                 start=(Dc == 0), stop=(Dc == ND - 1))
                    for Dc in range(ND):
                        mm = E.matmul(
                            bu[:, 0:512],
                            wub[:, sl * D + Dc * P:sl * D + (Dc + 1) * P],
                            xn_res5[:, Dc * D + g * 512:Dc * D + (g + 1) * 512],
                            start=(Dc == 0), stop=(Dc == ND - 1))
                    mm.then_inc(mm5.h, 2)
                ev("tensor", mmgu)
                mm5.n += 2
                osl = k % 2
                wait("scalar", mm5, mm5.n)
                if k >= 2:
                    wait("scalar", v5, k - 1)
                def a_si(E, hc=hc, bg=bg, osl=osl):
                    E.activation(out=silu_s[:, osl * 512:(osl + 1) * 512],
                                 in_=bg[:, 0:512], func=AF.Silu, bias=0.0,
                                 scale=sg_s[:, hc:hc + 1]).then_inc(a5.h, 1)
                ev("scalar", a_si)
                a5.n += 1
                wait("vector", a5, a5.n)
                if k >= 2:
                    wait("vector", g5, 16 * (k - 1))
                def v_h(E, hc=hc, bu=bu, osl=osl):
                    E.scalar_tensor_tensor(
                        out=h_stg[:, osl * 512:(osl + 1) * 512],
                        in0=bu[:, 0:512], scalar=su_s[:, hc:hc + 1],
                        in1=silu_s[:, osl * 512:(osl + 1) * 512],
                        op0=ALU.mult, op1=ALU.mult).then_inc(v5.h, 1)
                ev("vector", v_h)
                v5.n += 1
                wait("gpsimd", v5, v5.n)
                dma("gpsimd",
                    bass.AP(S_h.tensor, hc * P * S_OWN + g * 512,
                            [[S_OWN, P], [1, 512]]),
                    h_stg[:, osl * 512:(osl + 1) * 512], g5)
        if "hfm" in taps:
            gt = sem("gtap")
            wait("gpsimd", g5, g5.n)
            for i in range(NH):
                dma("gpsimd",
                    bass.AP(taps["hfm"].tensor, i * P * S_OWN,
                            [[S_OWN, P], [1, S_OWN]]),
                    bass.AP(S_h.tensor, i * P * S_OWN,
                            [[S_OWN, P], [1, S_OWN]]), gt)
        if stop_after == "ph5":
            wait("gpsimd", g5, g5.n)
            return _dispatch(nc, e, sched, max_events)

        arena.reset()
        wdi = arena.take(2 * HID, I32)
        wdb6 = arena.take(2 * HID, BF16)
        wait("sync", mm5, mm5.n)
        wait("sync", g5, g5.n)
        ld6 = [sem("ld60"), sem("ld61")]
        dq6 = sem("dq6")
        g6p = sem("g6p")
        for E_i in range(ND):
            sl = E_i % 2
            if E_i >= 2:
                wait("sync", dq6, E_i - 1)
            src = bass.AP(wdT.tensor, E_i * P, [[D, P], [P * D, NH], [1, P]])
            vld = dma("sync", wdi[:, sl * HID:(sl + 1) * HID], src, ld6[sl])
            wait("vector", ld6[sl], vld)
            if E_i >= 2:
                wait("vector", g6p, 16 * (E_i - 1))
            def dqd(E, sl=sl):
                E.tensor_scalar(out=wdb6[:, sl * HID:(sl + 1) * HID],
                                in0=wdi[:, sl * HID:(sl + 1) * HID],
                                scalar1=-128.0, scalar2=None,
                                op0=ALU.add).then_inc(dq6.h, 1)
            ev("vector", dqd)
            dq6.n += 1
            wait("gpsimd", dq6, dq6.n)
            dma("gpsimd",
                bass.AP(S_wd.tensor, E_i * P * HID, [[HID, P], [1, HID]]),
                wdb6[:, sl * HID:(sl + 1) * HID], g6p)

        arena.reset()
        hslab = arena.take(NH * 512, BF16)
        wdb = arena.take(2 * HID, BF16)
        attn_t = arena.take(2 * 512, BF16)
        sum_s = arena.take(2 * 512, BF16)
        l6h = sem("l6h")
        l6w = [sem("l6w0"), sem("l6w1")]
        l6a = [sem("l6a0"), sem("l6a1")]
        mm6 = sem("mm6")
        v6 = sem("v6")
        g6 = sem("g6")
        wait("sync", g6p, g6p.n)
        for grp in range(4):
            if grp > 0:
                wait("sync", mm6, grp * ND * NH)
            vh = dma("sync", hslab[:],
                     bass.AP(S_h.tensor, grp * 512,
                             [[S_OWN, P], [P * S_OWN, NH], [1, 512]]), l6h)
            for E_i in range(ND):
                sl = E_i % 2
                k = grp * ND + E_i
                if k >= 2:
                    wait("sync", mm6, NH * (k - 1))
                    wait("sync", v6, k - 1)
                vw_ = dma("sync", wdb[:, sl * HID:(sl + 1) * HID],
                          bass.AP(S_wd.tensor, E_i * P * HID,
                                  [[HID, P], [1, HID]]), l6w[sl])
                va_ = dma("sync", attn_t[:, sl * 512:(sl + 1) * 512],
                          bass.AP(S_attn.tensor, E_i * P * S_OWN + grp * 512,
                                  [[S_OWN, P], [1, 512]]), l6a[sl])
                if E_i == 0:
                    wait("tensor", l6h, vh)
                wait("tensor", l6w[sl], vw_)
                bank = PB[k % 2]
                if k >= 2:
                    wait("tensor", v6, k - 1)
                def mmd(E, sl=sl, bank=bank):
                    for hc in range(NH):
                        mm = E.matmul(
                            bank[:, 0:512],
                            wdb[:, sl * HID + hc * P:sl * HID + (hc + 1) * P],
                            hslab[:, hc * 512:(hc + 1) * 512],
                            start=(hc == 0), stop=(hc == NH - 1))
                    mm.then_inc(mm6.h, NH)
                ev("tensor", mmd)
                mm6.n += NH
                osl = k % 2
                wait("vector", mm6, mm6.n)
                wait("vector", l6a[sl], va_)
                if k >= 2:
                    wait("vector", g6, 16 * (k - 1))
                def v_d(E, E_i=E_i, sl=sl, bank=bank, osl=osl):
                    E.scalar_tensor_tensor(
                        out=sum_s[:, osl * 512:(osl + 1) * 512],
                        in0=bank[:, 0:512], scalar=sd_s[:, E_i:E_i + 1],
                        in1=attn_t[:, sl * 512:(sl + 1) * 512],
                        op0=ALU.mult, op1=ALU.add).then_inc(v6.h, 1)
                ev("vector", v_d)
                v6.n += 1
                wait("gpsimd", v6, v6.n)
                dma("gpsimd",
                    bass.AP(S_sum.tensor, E_i * P * S_OWN + grp * 512,
                            [[S_OWN, P], [1, 512]]),
                    sum_s[:, osl * 512:(osl + 1) * 512], g6)
        if stop_after == "ph6":
            wait("gpsimd", g6, g6.n)
            return _dispatch(nc, e, sched, max_events)

        arena.reset()
        sum_t = arena.take(2 * D, BF16)
        x7 = arena.take(2 * D, F32)
        o7 = arena.take(2 * D, F32)
        wait("sync", g6, g6.n)
        l7s = [sem("l7s0"), sem("l7s1")]
        l7x = [sem("l7x0"), sem("l7x1")]
        v7 = sem("v7")
        g7 = sem("g7")
        for t in range(NTO):
            sl = t % 2
            if t >= 2:
                wait("sync", v7, t - 1)
            vs_ = dma("sync", sum_t[:, sl * D:(sl + 1) * D],
                      bass.AP(S_sum.tensor, t * P, [[S_OWN, H * HD], [1, P]]),
                      l7s[sl], transpose=True)
            vx_ = dma("sync", x7[:, sl * D:(sl + 1) * D],
                      x_in[S_OWN + t * P:S_OWN + (t + 1) * P, :], l7x[sl])
            wait("vector", l7s[sl], vs_)
            wait("vector", l7x[sl], vx_)
            if t >= 2:
                wait("vector", g7, 16 * (t - 1))
            def v_o7(E, sl=sl):
                E.tensor_tensor(
                    out=o7[:, sl * D:(sl + 1) * D],
                    in0=sum_t[:, sl * D:(sl + 1) * D],
                    in1=x7[:, sl * D:(sl + 1) * D],
                    op=ALU.add).then_inc(v7.h, 1)
            ev("vector", v_o7)
            v7.n += 1
            wait("gpsimd", v7, v7.n)
            dma("gpsimd", out[t * P:(t + 1) * P, :],
                o7[:, sl * D:(sl + 1) * D], g7)
        wait("gpsimd", g7, g7.n)
        return _dispatch(nc, e, sched, max_events)

    return nc


def _dispatch(nc, e, sched, max_events=None):
    if max_events is not None:
        sched = sched[:max_events]
    block = e(nc.Block())
    vdr = e(nc.semaphore("vdr"))
    adr = e(nc.semaphore("adr"))
    gcell = e(nc.sbuf_tensor("gcell", [1, 8], mybir.dt.float32))

    def runner(key):
        def _(Eng):
            if key == "vector":
                Eng = Guarded(Eng, vdr, lambda E: E.engine_nop())
            elif key == "scalar":
                Eng.memzero(gcell[:])
                Eng = Guarded(Eng, adr, lambda E: E.activation(
                    out=gcell[0:1, 4:5], in_=gcell[0:1, 0:1],
                    func=AF.Copy, bias=0.0, scale=1.0))
            for k2, fn in sched:
                if k2 == key:
                    fn(Eng)
        return _
    block.sync(runner("sync"))
    block.vector(runner("vector"))
    block.scalar(runner("scalar"))
    block.tensor(runner("tensor"))
    block.gpsimd(runner("gpsimd"))
    return nc


def shard_inputs(inputs, core):
    b, phi = core // 2, core % 2
    xb = np.asarray(inputs["L0_attn_hidden_states"][b])
    own = xb[phi * S_OWN:(phi + 1) * S_OWN]
    other = xb[(1 - phi) * S_OWN:(2 - phi) * S_OWN]
    mn = np.asarray(inputs["L0_attn_memory_norms"])
    bf = ml_dtypes.bfloat16
    return {
        "x_in": np.ascontiguousarray(np.concatenate([other, own], 0)),
        "wqT": np.ascontiguousarray(inputs["L0_attn_w_q"].T),
        "wkT": np.ascontiguousarray(inputs["L0_attn_w_k"].T),
        "wvT": np.ascontiguousarray(inputs["L0_attn_w_v"].T),
        "woT": np.ascontiguousarray(inputs["L0_attn_w_o"].T),
        "mems": np.ascontiguousarray(inputs["L0_attn_memories"]),
        "mnT": np.ascontiguousarray(mn.transpose(2, 1, 0).reshape(128, 64)),
        "mn2": np.ascontiguousarray(mn.reshape(1, M * D)),
        "gate_row": np.ascontiguousarray(
            inputs["L0_attn_gate"].reshape(1, 16)),
        "wgT": np.ascontiguousarray(inputs["L0_ffn_wq_gate"].T),
        "wuT": np.ascontiguousarray(inputs["L0_ffn_wq_up"].T),
        "wdT": np.ascontiguousarray(inputs["L0_ffn_wq_down"].T),
        "sg_t": np.ascontiguousarray(
            inputs["L0_ffn_s_gate"].reshape(NH, 128).T),
        "su_t": np.ascontiguousarray(
            inputs["L0_ffn_s_up"].reshape(NH, 128).T),
        "sd_t": np.ascontiguousarray(
            inputs["L0_ffn_s_down"].reshape(ND, 128).T),
        "ident_bf": np.eye(P, dtype=bf),
        "ones_bf": np.ones((P, P), dtype=bf),
        "ones_f_in": np.ones((1, P), dtype=np.float32),
        "maskT": np.ascontiguousarray(
            np.tril(np.ones((P, P), np.float32)).T),
        "mask_col": np.full((P, 16), float(phi), np.float32),
    }


def kernel(**inputs):
    nc = build()
    in_maps = [shard_inputs(inputs, core) for core in range(8)]
    res = run_bass_kernel_spmd(nc, in_maps, core_ids=list(range(8)))
    outp = np.zeros((4, S_FULL, D), np.float32)
    for core in range(8):
        b, phi = core // 2, core % 2
        outp[b, phi * S_OWN:(phi + 1) * S_OWN] = \
            np.asarray(res.results[core]["out"])
    return outp
